# version 03
# baseline (speedup 1.0000x reference)
import numpy as np
import ml_dtypes
from contextlib import ExitStack

import concourse.bass as bass
import concourse.mybir as mybir
from concourse.bass_utils import run_bass_kernel_spmd

F32 = mybir.dt.float32
F16 = mybir.dt.float16
BF16 = mybir.dt.bfloat16
AF = mybir.ActivationFunctionType
ALU = mybir.AluOpType
AX = mybir.AxisListType

B, T, C = 4, 2048, 2048
HD = 256
CCn = 16
NTB = 4
EXT = (8, 16)
EPS = 1e-5
NCORE = 8


class Sem:
    def __init__(self, nc, es, name, dma=False):
        self.h = es.enter_context(nc.semaphore(name))
        self.v = 0
        self.step = 16 if dma else 1

    def inc(self, instr):
        instr.then_inc(self.h, self.step)
        self.v += self.step
        return instr

    def wait(self, eng, v=None):
        eng.wait_ge(self.h, self.v if v is None else v)


def build_program():
    nc = bass.Bass("TRN2", target_bir_lowering=False)

    x_tm = nc.dram_tensor("x_tm", [T, C], F32, kind="ExternalInput").ap()
    x_fm = nc.dram_tensor("x_fm", [C, T], F32, kind="ExternalInput").ap()
    x_own = nc.dram_tensor("x_own", [1024, C], F32, kind="ExternalInput").ap()
    xfm_own = nc.dram_tensor("xfm_own", [C, 1024], F32, kind="ExternalInput").ap()
    wq = nc.dram_tensor("wq", [C, 4096], F32, kind="ExternalInput").ap()
    wk = nc.dram_tensor("wk", [C, 2048], F32, kind="ExternalInput").ap()
    wv = nc.dram_tensor("wv", [C, 2048], F32, kind="ExternalInput").ap()
    wo = nc.dram_tensor("wo", [4096, C], F32, kind="ExternalInput").ap()
    wf = nc.dram_tensor("wf", [C, C], F32, kind="ExternalInput").ap()
    bias = nc.dram_tensor("bias", [1, C], F32, kind="ExternalInput").ap()
    cosk = nc.dram_tensor("cosk", [128, T], F32, kind="ExternalInput").ap()
    sink = nc.dram_tensor("sink", [128, T], F32, kind="ExternalInput").ap()
    cosq = nc.dram_tensor("cosq", [128, 1024], F32, kind="ExternalInput").ap()
    sinq = nc.dram_tensor("sinq", [128, 1024], F32, kind="ExternalInput").ap()
    masks = nc.dram_tensor("masks", [24, 128, 512], BF16, kind="ExternalInput").ap()
    ident16_d = nc.dram_tensor("ident16_d", [128, 128], F16, kind="ExternalInput").ap()
    ident32_d = nc.dram_tensor("ident32_d", [128, 128], F32, kind="ExternalInput").ap()
    out = nc.dram_tensor("out", [1024, C], F32, kind="ExternalOutput").ap()

    s_lnh = nc.dram_tensor("s_lnh", [CCn, 128, T], F16, kind="Internal").ap()
    s_lnl = nc.dram_tensor("s_lnl", [CCn, 128, T], F16, kind="Internal").ap()
    s_onh = nc.dram_tensor("s_onh", [CCn, 128, 1024], F16, kind="Internal").ap()
    s_onl = nc.dram_tensor("s_onl", [CCn, 128, 1024], F16, kind="Internal").ap()
    s_att = nc.dram_tensor("s_att", [1024, C], F32, kind="Internal").ap()

    x_tm_r = x_tm.rearrange("(tc p) c -> tc p c", p=128)
    x_own_r = x_own.rearrange("(tc p) c -> tc p c", p=128)
    x_fm_r = x_fm.rearrange("(cc p) t -> cc p t", p=128)
    xfm_own_r = xfm_own.rearrange("(cc p) t -> cc p t", p=128)
    wq_r = wq.rearrange("(cc p) j -> cc p j", p=128)
    wk_r = wk.rearrange("(cc p) j -> cc p j", p=128)
    wv_r = wv.rearrange("(cc p) j -> cc p j", p=128)
    wo_r = wo.rearrange("(qc p) j -> qc p j", p=128)
    wf_r = wf.rearrange("(oc p) j -> oc p j", p=128)
    out_r = out.rearrange("(tc p) c -> tc p c", p=128)
    s_att_r = s_att.rearrange("(tc q) c -> tc q c", q=128)

    es = ExitStack()
    sb = lambda name, shape, dt: es.enter_context(nc.sbuf_tensor(name, shape, dt))
    psb = lambda name, shape, dt: es.enter_context(nc.psum_tensor(name, shape, dt))

    ck = sb("ck", [128, T], F32)
    sk = sb("sk", [128, T], F32)
    cq = sb("cq", [128, 1024], F32)
    sq = sb("sq", [128, 1024], F32)
    ones32 = sb("ones32", [1, 128], F32)
    onesbf = sb("onesbf", [128, 1], BF16)
    wrg = [sb(f"wrg{i}", [128, 2048], F32) for i in range(2)]
    stats = sb("stats", [128, 24], F32)
    mv = sb("mv", [128, 2], F32)
    mr2 = sb("mr2", [128, 2], F32)
    sdt = sb("sdt", [128, 1], F32)
    epsc = sb("epsc", [128, 1], F32)

    pp = [psb(f"pp{i}", [128, 512], F32) for i in range(6)]
    pt16 = [psb(f"pt16_{i}", [128, 1024], F16) for i in range(2)]

    dC = Sem(nc, es, "dC", dma=True)
    dX = [Sem(nc, es, f"dX{i}", dma=True) for i in range(2)]
    dSp = [Sem(nc, es, f"dSp{i}", dma=True) for i in range(2)]
    dL = [Sem(nc, es, f"dL{i}", dma=True) for i in range(2)]
    dW = [Sem(nc, es, f"dW{i}", dma=True) for i in range(2)]
    dM = [Sem(nc, es, f"dM{i}", dma=True) for i in range(2)]
    dOb = Sem(nc, es, "dOb", dma=True)
    dA = [Sem(nc, es, f"dA{i}", dma=True) for i in range(2)]
    dOt = Sem(nc, es, "dOt", dma=True)
    ALLDMA = [dC, dX[0], dX[1], dSp[0], dSp[1], dL[0], dL[1],
              dW[0], dW[1], dM[0], dM[1], dOb, dA[0], dA[1], dOt]
    Tm = Sem(nc, es, "Tm")
    Vv = Sem(nc, es, "Vv")
    Sc = Sem(nc, es, "Sc")
    Gg = Sem(nc, es, "Gg")

    blk_cm = nc.Block()
    blk = blk_cm.__enter__()

    def SY(f): blk.sync(f)
    def TE(f): blk.tensor(f)

    def VE(f):
        vcur = Vv.v
        blk.vector(lambda v: (v.wait_ge(Vv.h, vcur), f(v)))

    def SE(f):
        scur = Sc.v
        blk.scalar(lambda ss_: (ss_.wait_ge(Sc.h, scur), f(ss_)))

    def GP(f): blk.gpsimd(f)

    def barrier():
        def _(v):
            Tm.wait(v); Sc.wait(v)
            for d in ALLDMA:
                d.wait(v)
            Vv.inc(v.engine_nop())
        VE(_)
        def _(s):
            Tm.wait(s); Vv.wait(s)
            for d in ALLDMA:
                d.wait(s)
            Sc.inc(s.nop())
        SE(_)
        def _(g):
            Tm.wait(g); Vv.wait(g); Sc.wait(g)
        GP(_)
        def _(s):
            Tm.wait(s); Vv.wait(s); Sc.wait(s)
        SY(_)

    def _(g):
        g.memset(ones32[:], 1.0)
        g.memset(epsc[:], EPS)
        Gg.inc(g.memset(onesbf[:], 1.0))
        dC.inc(g.dma_start(ck[:], cosk[:, :]))
        dC.inc(g.dma_start(sk[:], sink[:, :]))
        dC.inc(g.dma_start(cq[:], cosq[:, :]))
        dC.inc(g.dma_start(sq[:], sinq[:, :]))
    GP(_)

    xt_free = [[None], [None]]
    mv_free = [0]
    mr2_free = [0]

    p0 = ExitStack()
    sb0 = lambda name, shape, dt: p0.enter_context(nc.sbuf_tensor(name, shape, dt))
    xt = [sb0(f"xt{i}", [128, C], F32) for i in range(2)]
    ident32 = sb0("ident32", [128, 128], F32)
    lnt = sb0("lnt", [128, C], F32)
    lnh_t = [sb0(f"lnh_t{i}", [128, C], F16) for i in range(2)]
    lnl_t = [sb0(f"lnl_t{i}", [128, C], F16) for i in range(2)]
    m_rows = sb0("m_rows", [1, T], F32)
    r_rows = sb0("r_rows", [1, T], F32)
    mo_rows = sb0("mo_rows", [1, 1024], F32)
    ro_rows = sb0("ro_rows", [1, 1024], F32)
    m_bc = [sb0(f"m_bc{i}", [128, 512], F32) for i in range(NTB)]
    r_bc = [sb0(f"r_bc{i}", [128, 512], F32) for i in range(NTB)]
    mo_bc = [sb0(f"mo_bc{i}", [128, 512], F32) for i in range(2)]
    ro_bc = [sb0(f"ro_bc{i}", [128, 512], F32) for i in range(2)]
    lnt_free = [0]
    tr_free = [None]
    def _(g):
        dC.inc(g.dma_start(ident32[:], ident32_d[:, :]))
    GP(_)
    const0_mark = dC.v
    const_mark = dC.v

    def stats_pass(src_r, ntc, rows_dst):
        for tc in range(ntc):
            par = tc % 2
            def _(s, tc=tc, par=par):
                m = xt_free[par][0]
                if m is not None:
                    Vv.wait(s, m)
                dX[par].inc(s.dma_start(xt[par][:], src_r[tc]))
            SY(_)
            ld_mark = dX[par].v
            def _(v, par=par, ld_mark=ld_mark):
                dX[par].wait(v, ld_mark)
                Sc.wait(v, mv_free[0])
                for sg in range(4):
                    v.bn_stats(stats[:, 6 * sg:6 * sg + 6],
                               xt[par][:, 512 * sg:512 * (sg + 1)])
                Vv.inc(v.engine_nop())
                Vv.wait(v)
                Vv.inc(v.bn_aggr(mv[:], stats[:]))
            VE(_)
            xt_free[par][0] = Vv.v
            def _(s):
                Vv.wait(s)
                Gg.wait(s, 1)
                Sc.inc(s.activation(sdt[:], mv[:, 1:2], AF.Sqrt,
                                    bias=epsc[:, 0:1], scale=1.0))
            SE(_)
            mv_free[0] = Sc.v
            def _(v):
                Sc.wait(v)
                Tm.wait(v, mr2_free[0])
                v.reciprocal(mr2[:, 1:2], sdt[:])
                Vv.inc(v.tensor_copy(mr2[:, 0:1], mv[:, 0:1]))
            VE(_)
            def _(t):
                Vv.wait(t)
                dC.wait(t, const0_mark)
                if tr_free[0] is not None:
                    Vv.wait(t, tr_free[0])
                t.matmul(pp[4][0:1, 0:128], mr2[:, 0:1], ident32[:],
                         start=True, stop=True)
                Tm.inc(t.matmul(pp[4][0:1, 128:256], mr2[:, 1:2], ident32[:],
                                start=True, stop=True))
            TE(_)
            mr2_free[0] = Tm.v
            def _(v, tc=tc):
                Tm.wait(v)
                v.tensor_copy(rows_dst[0][:, tc * 128:(tc + 1) * 128],
                              pp[4][0:1, 0:128])
                Vv.inc(v.tensor_copy(rows_dst[1][:, tc * 128:(tc + 1) * 128],
                                     pp[4][0:1, 128:256]))
            VE(_)
            tr_free[0] = Vv.v

    stats_pass(x_tm_r, 16, (m_rows, r_rows))
    stats_pass(x_own_r, 8, (mo_rows, ro_rows))

    def bcast(rows, dst_m, dst_r, ntb):
        for tb in range(ntb):
            for which, dst in ((0, dst_m), (1, dst_r)):
                def _(t, tb=tb, which=which):
                    Vv.wait(t)
                    Gg.wait(t, 1)
                    Tm.inc(t.matmul(pp[5][:], ones32[:],
                                    rows[which][:, tb * 512:(tb + 1) * 512],
                                    start=True, stop=True))
                TE(_)
                def _(v, tb=tb, dst=dst):
                    Tm.wait(v)
                    Vv.inc(v.tensor_copy(dst[tb][:], pp[5][:]))
                VE(_)

    bcast((m_rows, r_rows), m_bc, r_bc, NTB)
    bcast((mo_rows, ro_rows), mo_bc, ro_bc, 2)

    spill_free = [[None], [None]]

    def ln_spill(src_r, dst_h, dst_l, mb, rb, ntb, width):
        for cc in range(CCn):
            par = cc % 2
            def _(s, cc=cc, par=par, width=width):
                m = xt_free[par][0]
                if m is not None:
                    Vv.wait(s, m)
                dX[par].inc(s.dma_start(xt[par][:, 0:width], src_r[cc]))
            SY(_)
            ld_mark = dX[par].v
            def _(v, par=par, ld_mark=ld_mark, ntb=ntb):
                dX[par].wait(v, ld_mark)
                Sc.wait(v, lnt_free[0])
                for tb in range(ntb):
                    sl = slice(tb * 512, (tb + 1) * 512)
                    v.tensor_tensor(lnt[:, sl], xt[par][:, sl], mb[tb][:], ALU.subtract)
                Vv.inc(v.engine_nop())
                Vv.wait(v)
                for tb in range(ntb):
                    sl = slice(tb * 512, (tb + 1) * 512)
                    v.tensor_tensor(lnt[:, sl], lnt[:, sl], rb[tb][:], ALU.mult)
                Vv.inc(v.engine_nop())
            VE(_)
            xt_free[par][0] = Vv.v
            lh, ll = lnh_t[par], lnl_t[par]
            def _(s, par=par, lh=lh, width=width):
                Vv.wait(s)
                m = spill_free[par][0]
                if m is not None:
                    dSp[par].wait(s, m)
                Sc.inc(s.copy(lh[:, 0:width], lnt[:, 0:width]))
            SE(_)
            lnt_free[0] = Sc.v
            def _(v, lh=lh, ll=ll, width=width):
                Sc.wait(v)
                Vv.inc(v.tensor_tensor(ll[:, 0:width], lnt[:, 0:width],
                                       lh[:, 0:width], ALU.subtract))
            VE(_)
            def _(g, cc=cc, par=par, lh=lh, ll=ll, width=width):
                Sc.wait(g)
                dSp[par].inc(g.dma_start(dst_h[cc], lh[:, 0:width]))
                Vv.wait(g)
                dSp[par].inc(g.dma_start(dst_l[cc], ll[:, 0:width]))
            GP(_)
            spill_free[par][0] = dSp[par].v

    ln_spill(x_fm_r, s_lnh, s_lnl, m_bc, r_bc, NTB, T)
    ln_spill(xfm_own_r, s_onh, s_onl, mo_bc, ro_bc, 2, 1024)
    spill_done = (dSp[0].v, dSp[1].v)

    barrier()
    p0.close()

    p1 = ExitStack()
    sb1 = lambda name, shape, dt: p1.enter_context(nc.sbuf_tensor(name, shape, dt))
    lgh = [sb1(f"lgh{i}", [128, 4 * 512], F16) for i in range(2)]
    lgl = [sb1(f"lgl{i}", [128, 4 * 512], F16) for i in range(2)]
    kph = sb1("kph", [128, 2 * T], F16)
    kpl = sb1("kpl", [128, 2 * T], F16)
    vph = sb1("vph", [128, CCn * 256], BF16)
    vpl = sb1("vpl", [128, CCn * 256], BF16)
    qph = sb1("qph", [128, 4 * 512], F16)
    qpl = sb1("qpl", [128, 4 * 512], F16)
    rt0 = sb1("rt0", [128, 512], F32)
    rt1 = sb1("rt1", [128, 512], F32)
    rf0 = sb1("rf0", [128, 512], F32)
    rf1 = sb1("rf1", [128, 512], F32)
    u_t = [sb1(f"u_t{i}", [128, 512], F32) for i in range(2)]
    s_t = [sb1(f"s_t{i}", [128, 512], F32) for i in range(2)]
    pf_t = [sb1(f"pf_t{i}", [128, 512], F32) for i in range(2)]
    ph_t = [sb1(f"ph_t{i}", [128, 512], BF16) for i in range(2)]
    pl_t = [sb1(f"pl_t{i}", [128, 512], BF16) for i in range(2)]
    zrow = sb1("zrow", [1, 512], F32)
    zinv = sb1("zinv", [1, 512], F32)
    zbc = sb1("zbc", [128, 512], F32)
    y32 = sb1("y32", [128, 512], F32)
    yh = [sb1(f"yh{i}", [128, 4 * 512], BF16) for i in range(2)]
    yl = [sb1(f"yl{i}", [128, 4 * 512], BF16) for i in range(2)]
    msk_t = [sb1(f"msk{i}", [128, 512], BF16) for i in range(2)]
    ob = sb1("ob", [128, 2048], F32)

    lnq_free = [[None], [None]]
    wrg_free = [[None], [None]]
    psum_free = {i: (0, 0) for i in range(6)}
    acc_mark = [0]
    ob_free = [0]
    kqv_free = [0]
    msk_mark_att = [0]
    rope_free_s = [0]
    wchunk_par = [0]
    msk_free = [0, 0]

    def load_w_chunk(dram_ap3, jw, ncc, dsth, dstl, col0, guard):
        par = wchunk_par[0]
        wchunk_par[0] = 1 - par
        def _(s, par=par):
            m = wrg_free[par][0]
            if m is not None:
                Vv.wait(s, m[0])
                Sc.wait(s, m[1])
            dW[par].inc(s.dma_start(
                wrg[par][:, 0:ncc * jw].rearrange("p (cc j) -> p cc j", j=jw),
                dram_ap3.transpose([1, 0, 2])))
        SY(_)
        ld_mark = dW[par].v
        def _(s, par=par, guard=guard):
            dW[par].wait(s, ld_mark)
            Tm.wait(s, guard)
            Sc.inc(s.copy(dsth[:, col0:col0 + ncc * jw], wrg[par][:, 0:ncc * jw]))
        SE(_)
        def _(v, par=par):
            Sc.wait(v)
            Vv.inc(v.tensor_tensor(dstl[:, col0:col0 + ncc * jw],
                                   wrg[par][:, 0:ncc * jw],
                                   dsth[:, col0:col0 + ncc * jw], ALU.subtract))
        VE(_)
        wrg_free[par][0] = (Vv.v, Sc.v)

    def rope_pair(psA, psB, cos_ap, sin_ap, dsth, dstl, colA, colB, tm_gate, tm_guard):
        def _(v, tm_gate=tm_gate):
            Tm.wait(v, tm_gate)
            dC.wait(v, const_mark)
            Sc.wait(v, rope_free_s[0])
            v.tensor_tensor(rt0[:], psA[:], cos_ap, ALU.mult)
            v.tensor_tensor(rt1[:], psB[:], sin_ap, ALU.mult)
            Vv.inc(v.engine_nop())
            Vv.wait(v)
            Vv.inc(v.tensor_tensor(rf0[:], rt0[:], rt1[:], ALU.subtract))
            Vv.wait(v)
            v.tensor_tensor(rt0[:], psB[:], cos_ap, ALU.mult)
            v.tensor_tensor(rt1[:], psA[:], sin_ap, ALU.mult)
            Vv.inc(v.engine_nop())
            Vv.wait(v)
            Vv.inc(v.tensor_tensor(rf1[:], rt0[:], rt1[:], ALU.add))
        VE(_)
        rf_mark = Vv.v
        def _(s, rf_mark=rf_mark, tm_guard=tm_guard):
            Vv.wait(s, rf_mark)
            Tm.wait(s, tm_guard)
            s.copy(dsth[:, colA:colA + 512], rf0[:])
            Sc.inc(s.copy(dsth[:, colB:colB + 512], rf1[:]))
        SE(_)
        rope_free_s[0] = Sc.v
        def _(v):
            Sc.wait(v)
            v.tensor_tensor(dstl[:, colA:colA + 512], rf0[:],
                            dsth[:, colA:colA + 512], ALU.subtract)
            Vv.inc(v.tensor_tensor(dstl[:, colB:colB + 512], rf1[:],
                                   dsth[:, colB:colB + 512], ALU.subtract))
        VE(_)

    def stream_sweep(kind, idx, wsplit_v, wsplit_s, wbh, wbl, wvh_=None, wvl_=None):
        if kind == "kv":
            src_h = s_lnh[:, :, idx * 512:(idx + 1) * 512]
            src_l = s_lnl[:, :, idx * 512:(idx + 1) * 512]
        else:
            src_h = s_onh[:, :, idx * 512:(idx + 1) * 512]
            src_l = s_onl[:, :, idx * 512:(idx + 1) * 512]
        for qtr in range(4):
            par = qtr % 2
            def _(s, par=par, qtr=qtr, src_h=src_h, src_l=src_l):
                m = lnq_free[par][0]
                if m is not None:
                    Tm.wait(s, m)
                else:
                    dSp[0].wait(s, spill_done[0])
                    dSp[1].wait(s, spill_done[1])
                dL[par].inc(s.dma_start(
                    lgh[par][:].rearrange("p (cc j) -> p cc j", j=512),
                    src_h[4 * qtr:4 * qtr + 4].transpose([1, 0, 2])))
                dL[par].inc(s.dma_start(
                    lgl[par][:].rearrange("p (cc j) -> p cc j", j=512),
                    src_l[4 * qtr:4 * qtr + 4].transpose([1, 0, 2])))
            SY(_)
            ln_mark = dL[par].v
            def _(t, par=par, qtr=qtr, ln_mark=ln_mark, kind=kind,
                  wsplit_v=wsplit_v, wsplit_s=wsplit_s):
                dL[par].wait(t, ln_mark)
                if qtr == 0:
                    Vv.wait(t, wsplit_v)
                    Sc.wait(t, wsplit_s)
                    nbank = 6 if kind == "kv" else 4
                    for bank in range(nbank):
                        m = psum_free[bank]
                        Vv.wait(t, m[0]); Sc.wait(t, m[1])
                ins = None
                if kind == "kv":
                    for dc in range(2):
                        for ccl in range(4):
                            cc = 4 * qtr + ccl
                            lh = lgh[par][:, ccl * 512:(ccl + 1) * 512]
                            ll = lgl[par][:, ccl * 512:(ccl + 1) * 512]
                            kh = wbh[:, cc * 256 + dc * 128: cc * 256 + (dc + 1) * 128]
                            kl = wbl[:, cc * 256 + dc * 128: cc * 256 + (dc + 1) * 128]
                            t.matmul(pp[dc][:], kh, lh, start=(cc == 0), stop=False)
                            t.matmul(pp[dc][:], kl, lh, start=False, stop=False)
                            ins = t.matmul(pp[dc][:], kh, ll, start=False,
                                           stop=(cc == CCn - 1))
                    for tcl in range(4):
                        bank = 2 + tcl
                        colo = 0
                        for ccl in range(4):
                            cc = 4 * qtr + ccl
                            lh = lgh[par][:, ccl * 512 + tcl * 128: ccl * 512 + (tcl + 1) * 128]
                            ll = lgl[par][:, ccl * 512 + tcl * 128: ccl * 512 + (tcl + 1) * 128]
                            vh = wvh_[:, cc * 256:(cc + 1) * 256]
                            vl = wvl_[:, cc * 256:(cc + 1) * 256]
                            t.matmul(pp[bank][:, colo:colo + 256], lh, vh,
                                     start=(cc == 0), stop=False)
                            t.matmul(pp[bank][:, colo:colo + 256], ll, vh,
                                     start=False, stop=False)
                            ins = t.matmul(pp[bank][:, colo:colo + 256], lh, vl,
                                           start=False, stop=(cc == CCn - 1))
                else:
                    for qdc in range(4):
                        for ccl in range(4):
                            cc = 4 * qtr + ccl
                            lh = lgh[par][:, ccl * 512:(ccl + 1) * 512]
                            ll = lgl[par][:, ccl * 512:(ccl + 1) * 512]
                            qh = wbh[:, cc * 512 + qdc * 128: cc * 512 + (qdc + 1) * 128]
                            ql = wbl[:, cc * 512 + qdc * 128: cc * 512 + (qdc + 1) * 128]
                            t.matmul(pp[qdc][:], qh, lh, start=(cc == 0), stop=False)
                            t.matmul(pp[qdc][:], ql, lh, start=False, stop=False)
                            ins = t.matmul(pp[qdc][:], qh, ll, start=False,
                                           stop=(cc == CCn - 1))
                Tm.inc(ins)
            TE(_)
            lnq_free[par][0] = Tm.v

    for p in range(8):
        guard = kqv_free[0]
        wsc = ExitStack()
        wkh = wsc.enter_context(nc.sbuf_tensor(f"wkh{p}", [128, CCn * 256], F16))
        wkl = wsc.enter_context(nc.sbuf_tensor(f"wkl{p}", [128, CCn * 256], F16))
        wvh = wsc.enter_context(nc.sbuf_tensor(f"wvh{p}", [128, CCn * 256], F16))
        wvl = wsc.enter_context(nc.sbuf_tensor(f"wvl{p}", [128, CCn * 256], F16))
        load_w_chunk(wk_r[0:8, :, p * 256:(p + 1) * 256], 256, 8, wkh, wkl, 0, guard)
        load_w_chunk(wk_r[8:16, :, p * 256:(p + 1) * 256], 256, 8, wkh, wkl, 2048, guard)
        load_w_chunk(wv_r[0:8, :, p * 256:(p + 1) * 256], 256, 8, wvh, wvl, 0, guard)
        load_w_chunk(wv_r[8:16, :, p * 256:(p + 1) * 256], 256, 8, wvh, wvl, 2048, guard)
        wsplit_v, wsplit_s = Vv.v, Sc.v

        for tb in range(4):
            stream_sweep("kv", tb, wsplit_v, wsplit_s, wkh, wkl, wvh, wvl)
            k_gate = Tm.v - 1
            tsl = slice(tb * 512, (tb + 1) * 512)
            rope_pair(pp[0], pp[1], ck[:, tsl], sk[:, tsl],
                      kph, kpl, 0 * T + tb * 512, 1 * T + tb * 512,
                      Tm.v, guard)
            psum_free[0] = (Vv.v, Sc.v)
            psum_free[1] = (Vv.v, Sc.v)
            def _(s, tb=tb, m=Tm.v, guard=guard):
                Tm.wait(s, m)
                Tm.wait(s, guard)
                ins = None
                for tcl in range(4):
                    bank = 2 + tcl
                    dst = slice((tb * 4 + tcl) * 256, (tb * 4 + tcl + 1) * 256)
                    ins = s.copy(vph[:, dst], pp[bank][:, 0:256])
                Sc.inc(ins)
            SE(_)
            def _(v, tb=tb):
                Sc.wait(v)
                ins = None
                for tcl in range(4):
                    bank = 2 + tcl
                    dst = slice((tb * 4 + tcl) * 256, (tb * 4 + tcl + 1) * 256)
                    ins = v.tensor_tensor(vpl[:, dst], pp[bank][:, 0:256],
                                          vph[:, dst], ALU.subtract)
                Vv.inc(ins)
            VE(_)
            for bank in range(2, 6):
                psum_free[bank] = (Vv.v, Sc.v)
        wsc.close()

        wsc = ExitStack()
        wqh = wsc.enter_context(nc.sbuf_tensor(f"wqh{p}", [128, CCn * 512], F16))
        wql = wsc.enter_context(nc.sbuf_tensor(f"wql{p}", [128, CCn * 512], F16))
        wq_guard = Tm.v
        for qq in range(4):
            load_w_chunk(wq_r[4 * qq:4 * qq + 4, :, p * 512:(p + 1) * 512],
                         512, 4, wqh, wql, qq * 4 * 512, wq_guard)
        wsplit_v, wsplit_s = Vv.v, Sc.v

        for slot in range(2):
            qgate = kqv_free[0] if slot == 0 else msk_mark_att[0]
            stream_sweep("q", slot, wsplit_v, wsplit_s, wqh, wql)
            qsl = slice(slot * 512, (slot + 1) * 512)
            for hh in range(2):
                rope_pair(pp[2 * hh], pp[2 * hh + 1], cq[:, qsl], sq[:, qsl],
                          qph, qpl, (2 * hh) * 512, (2 * hh + 1) * 512,
                          Tm.v, qgate)
            for bank in range(4):
                psum_free[bank] = (Vv.v, Sc.v)
            q_mark_v, q_mark_s = Vv.v, Sc.v

            nk = EXT[slot]
            moff = 0 if slot == 0 else EXT[0]
            for hh in range(2):
                u_free = [Vv.v, Vv.v]
                sp_free = [Sc.v, Sc.v]
                pf_free = [Vv.v, Vv.v]
                av_done = [Tm.v, Tm.v]
                tanh_done = [Sc.v, Sc.v]
                for kc in range(nk):
                    kpar = kc % 2
                    att_bank = 4 + kpar
                    mring = kc % 2
                    def _(s, kc=kc, mring=mring, moff=moff):
                        Vv.wait(s, msk_free[mring])
                        dM[mring].inc(s.dma_start(msk_t[mring][:], masks[moff + kc]))
                    SY(_)
                    mask_ld = dM[mring].v
                    def _(t, kc=kc, hh=hh, kpar=kpar, att_bank=att_bank,
                          q_mark_v=q_mark_v, q_mark_s=q_mark_s):
                        if kc < 2:
                            Vv.wait(t, q_mark_v)
                            Sc.wait(t, q_mark_s)
                            m = psum_free[att_bank]
                            Vv.wait(t, m[0]); Sc.wait(t, m[1])
                        else:
                            Sc.wait(t, tanh_done[kpar])
                        ins = None
                        for dc in range(2):
                            kh = kph[:, dc * T + kc * 128: dc * T + (kc + 1) * 128]
                            kl = kpl[:, dc * T + kc * 128: dc * T + (kc + 1) * 128]
                            qh = qph[:, (2 * hh + dc) * 512:(2 * hh + dc + 1) * 512]
                            ql = qpl[:, (2 * hh + dc) * 512:(2 * hh + dc + 1) * 512]
                            t.matmul(pp[att_bank][:], kh, qh, start=(dc == 0), stop=False)
                            t.matmul(pp[att_bank][:], kl, qh, start=False, stop=False)
                            ins = t.matmul(pp[att_bank][:], kh, ql, start=False,
                                           stop=(dc == 1))
                        Tm.inc(ins)
                    TE(_)
                    att_mark = Tm.v
                    def _(s, kpar=kpar, att_bank=att_bank, att_mark=att_mark):
                        Tm.wait(s, att_mark)
                        Vv.wait(s, u_free[kpar])
                        Sc.inc(s.activation(u_t[kpar][:], pp[att_bank][:], AF.Tanh,
                                            bias=0.0, scale=1.0 / 800.0))
                    SE(_)
                    tanh_done[kpar] = Sc.v
                    tanh_mark = Sc.v
                    def _(v, kpar=kpar, mring=mring, mask_ld=mask_ld, tanh_mark=tanh_mark):
                        Sc.wait(v, tanh_mark)
                        dM[mring].wait(v, mask_ld)
                        Sc.wait(v, sp_free[kpar])
                        Vv.inc(v.scalar_tensor_tensor(
                            s_t[kpar][:], u_t[kpar][:], 50.0,
                            msk_t[mring][:], ALU.mult, ALU.add))
                    VE(_)
                    u_free[kpar] = Vv.v
                    msk_free[mring] = Vv.v
                    st_mark = Vv.v
                    def _(s, kpar=kpar, st_mark=st_mark):
                        Vv.wait(s, st_mark)
                        Tm.wait(s, av_done[kpar])
                        Vv.wait(s, pf_free[kpar])
                        s.activation(pf_t[kpar][:], s_t[kpar][:], AF.Exp,
                                     bias=0.0, scale=1.0)
                        Sc.inc(s.activation(ph_t[kpar][:], s_t[kpar][:], AF.Exp,
                                            bias=0.0, scale=1.0))
                    SE(_)
                    exp_mark = Sc.v
                    sp_free[kpar] = Sc.v
                    def _(v, kpar=kpar, exp_mark=exp_mark):
                        Sc.wait(v, exp_mark)
                        Vv.inc(v.tensor_tensor(pl_t[kpar][:], pf_t[kpar][:],
                                               ph_t[kpar][:], ALU.subtract))
                    VE(_)
                    pf_free[kpar] = Vv.v
                    p_mark = Vv.v
                    def _(t, kc=kc, kpar=kpar, nk=nk, p_mark=p_mark):
                        Vv.wait(t, p_mark)
                        if kc == 0:
                            Gg.wait(t, 1)
                            for bk in (0, 1, 2, 3):
                                m = psum_free[bk]
                                Vv.wait(t, m[0]); Sc.wait(t, m[1])
                        last = (kc == nk - 1)
                        for dc in range(2):
                            vh = vph[:, kc * 256 + dc * 128: kc * 256 + (dc + 1) * 128]
                            vl = vpl[:, kc * 256 + dc * 128: kc * 256 + (dc + 1) * 128]
                            t.matmul(pp[dc][:], vh, ph_t[kpar][:],
                                     start=(kc == 0), stop=False)
                            t.matmul(pp[dc][:], vl, ph_t[kpar][:],
                                     start=False, stop=False)
                            t.matmul(pp[dc][:], vh, pl_t[kpar][:], start=False,
                                     stop=last)
                        t.matmul(pp[2][0:1, :], onesbf[:], ph_t[kpar][:],
                                 start=(kc == 0), stop=False)
                        ins = t.matmul(pp[2][0:1, :], onesbf[:], pl_t[kpar][:],
                                       start=False, stop=last)
                        Tm.inc(ins)
                    TE(_)
                    av_done[kpar] = Tm.v
                av_mark = Tm.v
                def _(v, av_mark=av_mark):
                    Tm.wait(v, av_mark)
                    Vv.inc(v.tensor_copy(zrow[:], pp[2][0:1, :]))
                    Vv.wait(v)
                    Vv.inc(v.reciprocal(zinv[:], zrow[:]))
                VE(_)
                def _(t):
                    Vv.wait(t)
                    m = psum_free[3]
                    Vv.wait(t, m[0]); Sc.wait(t, m[1])
                    Tm.inc(t.matmul(pp[3][:], ones32[:], zinv[:],
                                    start=True, stop=True))
                TE(_)
                def _(v):
                    Tm.wait(v)
                    Vv.inc(v.tensor_copy(zbc[:], pp[3][:]))
                VE(_)
                psum_free[2] = (Vv.v, Sc.v)
                psum_free[3] = (Vv.v, Sc.v)
                y32_free = [Sc.v]
                for dc in range(2):
                    ycol = slice((2 * hh + dc) * 512, (2 * hh + dc + 1) * 512)
                    def _(v, dc=dc):
                        Vv.wait(v)
                        Sc.wait(v, y32_free[0])
                        Vv.inc(v.tensor_tensor(y32[:], pp[dc][:], zbc[:], ALU.mult))
                    VE(_)
                    def _(s, ycol=ycol, slot=slot, guard=guard):
                        Vv.wait(s)
                        Tm.wait(s, guard)
                        Sc.inc(s.copy(yh[slot][:, ycol], y32[:]))
                    SE(_)
                    y32_free[0] = Sc.v
                    def _(v, ycol=ycol, slot=slot):
                        Sc.wait(v)
                        Vv.inc(v.tensor_tensor(yl[slot][:, ycol], y32[:],
                                               yh[slot][:, ycol], ALU.subtract))
                    VE(_)
                    psum_free[dc] = (Vv.v, Sc.v)
                psum_free[4] = (Vv.v, Sc.v)
                psum_free[5] = (Vv.v, Sc.v)
            msk_mark_att[0] = Tm.v
        wsc.close()

        wsc = ExitStack()
        woh = wsc.enter_context(nc.sbuf_tensor(f"woh{p}", [128, 4 * 2048], F16))
        wol = wsc.enter_context(nc.sbuf_tensor(f"wol{p}", [128, 4 * 2048], F16))
        wo_guard = Tm.v
        for qdc in range(4):
            load_w_chunk(wo_r[p * 4 + qdc: p * 4 + qdc + 1], 2048, 1,
                         woh, wol, qdc * 2048, wo_guard)
        wo_v, wo_s = Vv.v, Sc.v
        for slot in range(2):
            for tcl in range(4):
                def _(t, slot=slot, tcl=tcl, wo_v=wo_v, wo_s=wo_s):
                    Vv.wait(t, wo_v)
                    Sc.wait(t, wo_s)
                    for bk in range(4):
                        m = psum_free[bk]
                        Vv.wait(t, m[0]); Sc.wait(t, m[1])
                    ins = None
                    for qdc in range(4):
                        yhs = yh[slot][:, qdc * 512 + tcl * 128: qdc * 512 + (tcl + 1) * 128]
                        yls = yl[slot][:, qdc * 512 + tcl * 128: qdc * 512 + (tcl + 1) * 128]
                        for cb in range(4):
                            wos = woh[:, qdc * 2048 + cb * 512: qdc * 2048 + (cb + 1) * 512]
                            wls = wol[:, qdc * 2048 + cb * 512: qdc * 2048 + (cb + 1) * 512]
                            t.matmul(pp[cb][:], yhs, wos, start=(qdc == 0), stop=False)
                            t.matmul(pp[cb][:], yls, wos, start=False, stop=False)
                            ins = t.matmul(pp[cb][:], yhs, wls, start=False,
                                           stop=(qdc == 3))
                    Tm.inc(ins)
                TE(_)
                op_mark = Tm.v
                def _(v, op_mark=op_mark):
                    Tm.wait(v, op_mark)
                    dOb.wait(v, ob_free[0])
                    ins = None
                    for cb in range(4):
                        ins = v.tensor_copy(ob[:, cb * 512:(cb + 1) * 512], pp[cb][:])
                    Vv.inc(ins)
                VE(_)
                for bk in range(4):
                    psum_free[bk] = (Vv.v, Sc.v)
                def _(g, p=p, slot=slot, tcl=tcl):
                    Vv.wait(g)
                    if p > 0:
                        dOb.wait(g, acc_mark[0])
                    if p == 0:
                        ins = g.dma_start(s_att_r[slot * 4 + tcl], ob[:])
                    else:
                        ins = g.dma_start(s_att_r[slot * 4 + tcl], ob[:],
                                          accum_op=ALU.add)
                    dOb.inc(ins)
                GP(_)
                ob_free[0] = dOb.v
        wsc.close()
        acc_mark[0] = dOb.v
        kqv_free[0] = Tm.v

    acc_done = dOb.v
    barrier()
    p1.close()

    p2 = ExitStack()
    sb2 = lambda name, shape, dt: p2.enter_context(nc.sbuf_tensor(name, shape, dt))
    xt = [sb2(f"xt2_{i}", [128, C], F32) for i in range(2)]
    bias_bc = sb2("bias_bc", [128, C], F32)
    ident = sb2("ident", [128, 128], F16)
    qT = sb2("qT", [128, CCn * 2048], F16)
    def _(g):
        dC.inc(g.dma_start(bias_bc[:], bias[0:1, :].broadcast_to((128, C))))
        dC.inc(g.dma_start(ident[:], ident16_d[:, :]))
    GP(_)
    const2_mark = dC.v
    xt_free = [[None], [None]]
    thr = sb2("thr", [128, 1], F32)
    thr2 = sb2("thr2", [128, 1], F32)
    nthr = sb2("nthr", [128, 1], F32)
    g_t = sb2("g_t", [128, 2048], F16)
    l_t = sb2("l_t", [128, 2048], F16)
    q_t = sb2("q_t", [128, 2048], F16)
    at_t = [sb2(f"at_t{i}", [128, C], F32) for i in range(2)]
    x2_t = sb2("x2_t", [128, C], F32)
    ln2_t = sb2("ln2_t", [128, C], F32)
    ln2h = sb2("ln2h", [128, C], F16)
    ln2l = sb2("ln2l", [128, C], F16)
    l2fh = sb2("l2fh", [128, CCn * 128], F16)
    l2fl = sb2("l2fl", [128, CCn * 128], F16)
    fb_t = sb2("fb_t", [128, C], F32)
    ot_t = sb2("ot_t", [128, C], F32)

    pt_free = [0]
    q_free = [0]
    for oc in range(CCn):
        par = oc % 2
        def _(s, oc=oc, par=par):
            m = wrg_free[par][0]
            if m is not None:
                Vv.wait(s, m[0])
                Sc.wait(s, m[1])
            dW[par].inc(s.dma_start(wrg[par][:], wf_r[oc]))
        SY(_)
        ld_mark = dW[par].v
        def _(v, par=par, ld_mark=ld_mark):
            dW[par].wait(v, ld_mark)
            Tm.wait(v, q_free[0])
            v.tensor_reduce(thr[:], wrg[par][:], AX.X, ALU.add,
                            apply_absolute_value=True)
            Vv.inc(v.engine_nop())
            Vv.wait(v)
            v.tensor_scalar_mul(thr2[:], thr[:], 0.7 / 2048.0)
            Vv.inc(v.engine_nop())
            Vv.wait(v)
            v.tensor_scalar_mul(nthr[:], thr2[:], -1.0)
            Vv.inc(v.engine_nop())
            Vv.wait(v)
            v.tensor_scalar(g_t[:], wrg[par][:], thr2[:], None, ALU.is_gt)
            v.tensor_scalar(l_t[:], wrg[par][:], nthr[:], None, ALU.is_lt)
            Vv.inc(v.engine_nop())
            Vv.wait(v)
            Vv.inc(v.tensor_tensor(q_t[:], g_t[:], l_t[:], ALU.subtract))
        VE(_)
        wrg_free[par][0] = (Vv.v, Sc.v)
        q_mark = Vv.v
        def _(t, q_mark=q_mark):
            Vv.wait(t, q_mark)
            Vv.wait(t, pt_free[0])
            dC.wait(t, const2_mark)
            ins = None
            for cc in range(CCn):
                bank = pt16[cc // 8]
                ins = t.transpose(bank[:, (cc % 8) * 128:(cc % 8 + 1) * 128],
                                  q_t[:, cc * 128:(cc + 1) * 128], ident[:])
            Tm.inc(ins)
        TE(_)
        q_free[0] = Tm.v
        tr_mark = Tm.v
        def _(v, oc=oc, tr_mark=tr_mark):
            Tm.wait(v, tr_mark)
            ins = None
            for cc in range(CCn):
                bank = pt16[cc // 8]
                ins = v.tensor_copy(qT[:, cc * 2048 + oc * 128: cc * 2048 + (oc + 1) * 128],
                                    bank[:, (cc % 8) * 128:(cc % 8 + 1) * 128])
            Vv.inc(ins)
        VE(_)
        pt_free[0] = Vv.v
    qT_mark = Vv.v

    at_free = [[None], [None]]
    ot_free = [None]
    for tc in range(8):
        par = tc % 2
        def _(g, tc=tc, par=par):
            dOb.wait(g, acc_done)
            m = at_free[par][0]
            if m is not None:
                Vv.wait(g, m)
            m2 = xt_free[par][0]
            if m2 is not None:
                Vv.wait(g, m2)
            dA[par].inc(g.dma_start(at_t[par][:], s_att_r[tc]))
            dA[par].inc(g.dma_start(xt[par][:], x_own_r[tc]))
        GP(_)
        at_mark = dA[par].v
        def _(v, par=par, at_mark=at_mark):
            dA[par].wait(v, at_mark)
            Sc.wait(v, mv_free[0])
            for sg in range(4):
                v.bn_stats(stats[:, 6 * sg:6 * sg + 6],
                           at_t[par][:, 512 * sg:512 * (sg + 1)])
            Vv.inc(v.engine_nop())
            Vv.wait(v)
            Vv.inc(v.bn_aggr(mv[:], stats[:]))
        VE(_)
        def _(s):
            Vv.wait(s)
            Gg.wait(s, 1)
            Sc.inc(s.activation(sdt[:], mv[:, 1:2], AF.Sqrt, bias=epsc[:, 0:1], scale=1.0))
        SE(_)
        mv_free[0] = Sc.v
        def _(v, par=par):
            Sc.wait(v)
            Vv.inc(v.reciprocal(mr2[:, 1:2], sdt[:]))
            Vv.wait(v)
            Vv.inc(v.tensor_scalar(ln2_t[:], at_t[par][:], mv[:, 0:1], mr2[:, 1:2],
                                   ALU.subtract, ALU.mult))
            Vv.wait(v)
            Vv.inc(v.tensor_tensor(x2_t[:], ln2_t[:], xt[par][:], ALU.add))
        VE(_)
        at_free[par][0] = Vv.v
        xt_free[par][0] = Vv.v
        def _(v):
            Vv.wait(v)
            Sc.wait(v, mv_free[0])
            for sg in range(4):
                v.bn_stats(stats[:, 6 * sg:6 * sg + 6],
                           x2_t[:, 512 * sg:512 * (sg + 1)])
            Vv.inc(v.engine_nop())
            Vv.wait(v)
            Vv.inc(v.bn_aggr(mv[:], stats[:]))
        VE(_)
        def _(s):
            Vv.wait(s)
            Gg.wait(s, 1)
            Sc.inc(s.activation(sdt[:], mv[:, 1:2], AF.Sqrt, bias=epsc[:, 0:1], scale=1.0))
        SE(_)
        mv_free[0] = Sc.v
        def _(v):
            Sc.wait(v)
            Vv.inc(v.reciprocal(mr2[:, 1:2], sdt[:]))
            Vv.wait(v)
            Vv.inc(v.tensor_scalar(ln2_t[:], x2_t[:], mv[:, 0:1], mr2[:, 1:2],
                                   ALU.subtract, ALU.mult))
        VE(_)
        def _(s):
            Vv.wait(s)
            Sc.inc(s.copy(ln2h[:], ln2_t[:]))
        SE(_)
        def _(v):
            Sc.wait(v)
            Vv.inc(v.tensor_tensor(ln2l[:], ln2_t[:], ln2h[:], ALU.subtract))
        VE(_)
        ln2_v, ln2_s = Vv.v, Sc.v
        for rnd in range(2):
            def _(t, rnd=rnd, ln2_v=ln2_v, ln2_s=ln2_s):
                Vv.wait(t, ln2_v)
                Sc.wait(t, ln2_s)
                Vv.wait(t, pt_free[0])
                dC.wait(t, const2_mark)
                ins = None
                for j in range(8):
                    cc = rnd * 8 + j
                    t.transpose(pt16[0][:, j * 128:(j + 1) * 128],
                                ln2h[:, cc * 128:(cc + 1) * 128], ident[:])
                    ins = t.transpose(pt16[1][:, j * 128:(j + 1) * 128],
                                      ln2l[:, cc * 128:(cc + 1) * 128], ident[:])
                Tm.inc(ins)
            TE(_)
            tr_mark = Tm.v
            def _(v, rnd=rnd, tr_mark=tr_mark):
                Tm.wait(v, tr_mark)
                ins = None
                for j in range(8):
                    cc = rnd * 8 + j
                    v.tensor_copy(l2fh[:, cc * 128:(cc + 1) * 128],
                                  pt16[0][:, j * 128:(j + 1) * 128])
                    ins = v.tensor_copy(l2fl[:, cc * 128:(cc + 1) * 128],
                                        pt16[1][:, j * 128:(j + 1) * 128])
                Vv.inc(ins)
            VE(_)
            pt_free[0] = Vv.v
        l2f_mark = Vv.v
        def _(t, l2f_mark=l2f_mark):
            Vv.wait(t, max(l2f_mark, qT_mark))
            for bk in range(4):
                m = psum_free[bk]
                Vv.wait(t, m[0]); Sc.wait(t, m[1])
            for ob_i in range(4):
                ins = None
                for cc in range(CCn):
                    lh = l2fh[:, cc * 128:(cc + 1) * 128]
                    ll = l2fl[:, cc * 128:(cc + 1) * 128]
                    qs = qT[:, cc * 2048 + ob_i * 512: cc * 2048 + (ob_i + 1) * 512]
                    t.matmul(pp[ob_i][:], lh, qs, start=(cc == 0), stop=False)
                    ins = t.matmul(pp[ob_i][:], ll, qs, start=False,
                                   stop=(cc == CCn - 1))
                Tm.inc(ins)
        TE(_)
        ffn_mark = Tm.v
        def _(v, ffn_mark=ffn_mark):
            Tm.wait(v, ffn_mark)
            dC.wait(v, const2_mark)
            Sc.wait(v, mv_free[0])
            ins = None
            for ob_i in range(4):
                ins = v.tensor_tensor(fb_t[:, ob_i * 512:(ob_i + 1) * 512], pp[ob_i][:],
                                      bias_bc[:, ob_i * 512:(ob_i + 1) * 512], ALU.add)
            Vv.inc(ins)
            Vv.wait(v)
            for sg in range(4):
                v.bn_stats(stats[:, 6 * sg:6 * sg + 6],
                           fb_t[:, 512 * sg:512 * (sg + 1)])
            Vv.inc(v.engine_nop())
            Vv.wait(v)
            Vv.inc(v.bn_aggr(mv[:], stats[:]))
        VE(_)
        for bk in range(4):
            psum_free[bk] = (Vv.v, Sc.v)
        def _(s):
            Vv.wait(s)
            Gg.wait(s, 1)
            Sc.inc(s.activation(sdt[:], mv[:, 1:2], AF.Sqrt, bias=epsc[:, 0:1], scale=1.0))
        SE(_)
        mv_free[0] = Sc.v
        def _(v, par=par):
            Sc.wait(v)
            if ot_free[0] is not None:
                dOt.wait(v, ot_free[0])
            Vv.inc(v.reciprocal(mr2[:, 1:2], sdt[:]))
            Vv.wait(v)
            Vv.inc(v.tensor_scalar(ln2_t[:], fb_t[:], mv[:, 0:1], mr2[:, 1:2],
                                   ALU.subtract, ALU.mult))
            Vv.wait(v)
            Vv.inc(v.tensor_tensor(ot_t[:], ln2_t[:], x2_t[:], ALU.add))
        VE(_)
        def _(g, tc=tc):
            Vv.wait(g)
            dOt.inc(g.dma_start(out_r[tc], ot_t[:]))
        GP(_)
        ot_free[0] = dOt.v

    def _(g):
        g.wait_ge(dOt.h, dOt.v)
    GP(_)

    p2.close()
    blk_cm.__exit__(None, None, None)
    es.close()
    return nc


_NC_CACHE = {}


def kernel(L0_attn_X, L0_attn_Wq, L0_attn_Wk, L0_attn_Wv, L0_attn_Wo,
           L0_ffn_weight, L0_ffn_bias):
    X = np.ascontiguousarray(np.asarray(L0_attn_X, dtype=np.float32))
    Wq = np.ascontiguousarray(np.asarray(L0_attn_Wq, dtype=np.float32))
    Wk = np.ascontiguousarray(np.asarray(L0_attn_Wk, dtype=np.float32))
    Wv = np.ascontiguousarray(np.asarray(L0_attn_Wv, dtype=np.float32))
    Wo = np.ascontiguousarray(np.asarray(L0_attn_Wo, dtype=np.float32))
    Wf = np.ascontiguousarray(np.asarray(L0_ffn_weight, dtype=np.float32))
    bias = np.ascontiguousarray(np.asarray(L0_ffn_bias, dtype=np.float32)).reshape(1, C)

    inv_freq = 1.0 / (10000.0 ** (np.arange(0, HD, 2, dtype=np.float64) / HD))
    ang = inv_freq[:, None] * np.arange(T, dtype=np.float64)[None, :]
    coskt = np.cos(ang).astype(np.float32)
    sinkt = np.sin(ang).astype(np.float32)

    if "nc" not in _NC_CACHE:
        _NC_CACHE["nc"] = build_program()
    nc = _NC_CACHE["nc"]

    in_maps = []
    owns = []
    for c in range(NCORE):
        b, h = c // 2, c % 2
        blocks = (0, 3) if h == 0 else (1, 2)
        own = np.concatenate([np.arange(blocks[0] * 512, (blocks[0] + 1) * 512),
                              np.arange(blocks[1] * 512, (blocks[1] + 1) * 512)])
        owns.append((b, own))
        xb = X[b]
        x_fmc = np.ascontiguousarray(xb.T)
        mask_arr = np.full((24, 128, 512), -1e30, dtype=np.float32)
        for s in range(2):
            qs = blocks[s] * 512
            base = 0 if s == 0 else EXT[0]
            for kc in range(EXT[s]):
                kabs = kc * 128 + np.arange(128)[:, None]
                qabs = qs + np.arange(512)[None, :]
                mask_arr[base + kc] = np.where(kabs <= qabs, 0.0, -1e30)
        in_maps.append({
            "x_tm": xb,
            "x_fm": x_fmc,
            "x_own": np.ascontiguousarray(xb[own]),
            "xfm_own": np.ascontiguousarray(x_fmc[:, own]),
            "wq": Wq, "wk": Wk, "wv": Wv, "wo": Wo, "wf": Wf, "bias": bias,
            "cosk": coskt, "sink": sinkt,
            "cosq": np.ascontiguousarray(coskt[:, own]),
            "sinq": np.ascontiguousarray(sinkt[:, own]),
            "masks": mask_arr.astype(ml_dtypes.bfloat16),
            "ident16_d": np.eye(128, dtype=np.float16),
            "ident32_d": np.eye(128, dtype=np.float32),
        })

    res = run_bass_kernel_spmd(nc, in_maps, core_ids=list(range(NCORE)))
    outp = np.empty((B, T, C), dtype=np.float32)
    for c in range(NCORE):
        b, own = owns[c]
        outp[b, own] = res.results[c]["out"]
    return outp
